# version 05
# baseline (speedup 1.0000x reference)
import os
import numpy as np
import ml_dtypes
import concourse.bass as bass
import concourse.bass_isa as bass_isa
import concourse.mybir as mybir
from concourse.bass_utils import run_bass_kernel_spmd

F32 = mybir.dt.float32
BF16 = mybir.dt.bfloat16
AX = mybir.AxisListType.X
ALU = mybir.AluOpType
ACT = mybir.ActivationFunctionType

NTOK, D, OWN, INNER = 4096, 1024, 2048, 4096
NTILE, OTILE = 32, 16
NF, N1, N2, K1H, NPAIR = 8192, 128, 64, 65, 33
CCH = 256
NCHUNK = D // CCH
LN_EPS = 1e-5
RPE_L, FEAT = 3, 32
BIGC = np.float32(1.5 * 2 ** 23)
K32C = np.float32(3.0 * (2 ** 52 - 2 ** 23))
DBG = [s for s in os.environ.get("KDBG", "").split(",") if s]
PHASE_LIMIT = int(os.environ.get("KPHASES", "99"))
KSKIP = set(int(x) for x in os.environ.get("KSKIP", "").split(",") if x)

ENGINES = ("sync", "scalar", "tensor", "vector", "gpsimd")


class P:
    def __init__(self):
        self.ops = []
        self.last = {e: None for e in ENGINES}
        self.serial = False

    def add(self, engine, fn, deps=(), dma=False, small=False):
        deps = [d for d in deps if d is not None]
        if self.serial:
            deps = deps + [v for v in self.last.values() if v is not None]
            small = True
        self.ops.append(dict(e=engine, fn=fn, deps=list(deps), dma=dma, small=small))
        self.last[engine] = len(self.ops) - 1
        return len(self.ops) - 1

    def barrier(self):
        prevs = [v for v in self.last.values() if v is not None]
        for e in ENGINES:
            self.add(e, lambda eng: eng.nop(), deps=list(prevs), small=True)

    def lower(self, sems_c, sems_d):
        n = len(self.ops)
        needed = [op["dma"] for op in self.ops]
        for op in self.ops:
            for d in op["deps"]:
                o = self.ops[d]
                if o["e"] != op["e"] or o["small"] or o["dma"]:
                    needed[d] = True
        val = [0] * n
        ccount = {e: 0 for e in ENGINES}
        dcount = {e: 0 for e in ENGINES}
        for i, op in enumerate(self.ops):
            if needed[i]:
                if op["dma"]:
                    dcount[op["e"]] += 16
                    val[i] = dcount[op["e"]]
                else:
                    ccount[op["e"]] += 1
                    val[i] = ccount[op["e"]]

        def emit_engine(en, eng):
            for i, op in enumerate(self.ops):
                if op["e"] != en:
                    continue
                waits = {}
                for d in op["deps"]:
                    o = self.ops[d]
                    if o["e"] == en and not (o["small"] or o["dma"]):
                        continue
                    waits[(o["e"], o["dma"])] = max(waits.get((o["e"], o["dma"]), 0), val[d])
                for (e2, isd), v in sorted(waits.items(), key=lambda kv: str(kv[0])):
                    eng.wait_ge(sems_d[e2] if isd else sems_c[e2], v)
                ins = op["fn"](eng)
                if needed[i]:
                    ins.then_inc(sems_d[en] if op["dma"] else sems_c[en], 16 if op["dma"] else 1)
        return emit_engine, dcount, ccount


def _bf(x):
    return np.asarray(x, np.float32).astype(ml_dtypes.bfloat16)


def build_fft_tables():
    j1, k1 = np.arange(64), np.arange(K1H)
    C1 = np.cos(2 * np.pi * np.outer(j1, k1) / N1)
    S1 = -np.sin(2 * np.pi * np.outer(j1, k1) / N1)
    j2 = np.arange(N2)
    k2 = np.arange(N2)
    MB = np.zeros((NPAIR, 3, 128, 128))
    MBi = np.zeros((NPAIR, 3, 128, 128))
    for pp in range(NPAIR):
        for hh in range(2):
            q = 2 * pp + hh
            if q >= K1H:
                continue
            tw = np.exp(-2j * np.pi * j2 * q / NF)
            M = tw[:, None] * np.exp(-2j * np.pi * np.outer(j2, k2) / N2)
            Mi_ = np.conj(M).T
            sl = slice(64 * hh, 64 * hh + 64)
            MB[pp, 0, sl, sl], MB[pp, 1, sl, sl], MB[pp, 2, sl, sl] = M.real, M.imag, -M.imag
            MBi[pp, 0, sl, sl], MBi[pp, 1, sl, sl], MBi[pp, 2, sl, sl] = (
                Mi_.real, Mi_.imag, -Mi_.imag)
    ang = 2 * np.pi * np.outer(k1, np.arange(64)) / N1
    alpha = np.where((k1 == 0) | (k1 == 64), 1.0, 2.0)[:, None]
    CA, SA = alpha * np.cos(ang), -alpha * np.sin(ang)
    return C1, S1, MB, MBi, CA, SA


def build_program():
    nc = bass.Bass("TRN2", target_bir_lowering=False, detect_race_conditions=False)
    p = P()

    ein = lambda name, shape, dt: nc.dram_tensor(name, shape, dt, kind="ExternalInput").ap()
    x_full = ein("x_full", [NTOK, D], F32)
    x_own = ein("x_own", [OWN, D], F32)
    wuv = ein("wuv", [D, 2 * D], F32)
    wo = ein("wo", [D, D], F32)
    rpe_in = ein("rpe_in", [1, FEAT], F32)
    rpe_hid = ein("rpe_hid", [RPE_L * FEAT, FEAT], F32)
    rpe_g = ein("rpe_g", [RPE_L, FEAT], F32)
    rpe_b = ein("rpe_b", [RPE_L, FEAT], F32)
    rpe_out = ein("rpe_out", [FEAT, D], F32)
    dgamma = ein("dgamma", [1, D], F32)
    wg_in = ein("wg_in", [INNER, D], F32)
    wu_in = ein("wu_in", [INNER, D], F32)
    wd_in = ein("wd_in", [D, INNER], F32)
    out_sh = nc.dram_tensor("out_sh", [OWN, D], F32, kind="ExternalOutput").ap()

    def scratch(name, shape, dt):
        kind = "ExternalOutput" if name in DBG else "Internal"
        return nc.dram_tensor(name, shape, dt, kind=kind).ap()

    v_hbm = scratch("v_hbm", [NTOK, D], BF16)
    t_hbm = scratch("t_hbm", [NTOK, D], BF16)
    u_hbm = scratch("u_hbm", [OWN, D], BF16)
    o_hbm = scratch("o_hbm", [NTOK + 128, D], BF16)
    xqT_hbm = scratch("xqT_hbm", [D, OWN], BF16)
    wgT_hbm = scratch("wgT_hbm", [D, INNER], BF16)
    wuT_hbm = scratch("wuT_hbm", [D, INNER], BF16)
    wdT_hbm = scratch("wdT_hbm", [INNER, D], BF16)
    A_hbm = scratch("A_hbm", [66, N2 * 2 * CCH], BF16)
    E_hbm = scratch("E_hbm", [66, N2 * 2 * CCH], BF16)
    ffn_hbm = scratch("ffn_hbm", [OWN, D], F32)
    mbb_hbm = scratch("mbb_hbm", [NPAIR, 2 * 3 * 128 * 128], BF16)
    mbib_hbm = scratch("mbib_hbm", [NPAIR, 2 * 3 * 128 * 128], BF16)
    h_dbg = scratch("h_dbg", [34, NTOK], F32)
    that_dbg = scratch("that_dbg", [128, NPAIR * 2 * CCH], BF16)

    C1T, S1T, MBT, MBiT, CAT, SAT = build_fft_tables()
    _lo = lambda T: np.asarray(np.asarray(T, np.float64)
                               - np.asarray(np.asarray(T, np.float32), np.float64),
                               np.float32)
    MBflat = MBT.transpose(0, 2, 1, 3).reshape(NPAIR, 3 * 128 * 128)
    MBiflat = MBiT.transpose(0, 2, 1, 3).reshape(NPAIR, 3 * 128 * 128)
    c1_d = nc.inline_tensor(np.asarray(C1T, np.float32), name="c1_t").ap()
    s1_d = nc.inline_tensor(np.asarray(S1T, np.float32), name="s1_t").ap()
    c1lo_d = nc.inline_tensor(_lo(C1T), name="c1lo_t").ap()
    s1lo_d = nc.inline_tensor(_lo(S1T), name="s1lo_t").ap()
    mb_d = nc.inline_tensor(np.asarray(MBflat, np.float32), name="mb_t").ap()
    mbi_d = nc.inline_tensor(np.asarray(MBiflat, np.float32), name="mbi_t").ap()
    mblo_d = nc.inline_tensor(_lo(MBflat), name="mblo_t").ap()
    mbilo_d = nc.inline_tensor(_lo(MBiflat), name="mbilo_t").ap()
    ca_d = nc.inline_tensor(np.asarray(CAT, np.float32), name="ca_t").ap()
    sa_d = nc.inline_tensor(np.asarray(SAT, np.float32), name="sa_t").ap()
    calo_d = nc.inline_tensor(_lo(CAT), name="calo_t").ap()
    salo_d = nc.inline_tensor(_lo(SAT), name="salo_t").ap()
    ident_d = nc.inline_tensor(np.eye(128, dtype=np.float32), name="ident_t").ap()
    identb_d = nc.inline_tensor(_bf(np.eye(128)), name="identb_t").ap()
    aux = np.zeros((128, 12), np.float32)
    aux[:, 0], aux[:, 1], aux[:, 2], aux[:, 3] = BIGC, K32C, LN_EPS, 1.0 / FEAT
    aux[:, 4], aux[:, 5], aux[:, 6], aux[:, 7] = np.log(1.0 / NF), 0.99, 0.01, 1.0
    _lod = lambda v: np.float32(np.float64(v) - np.float64(np.float32(v)))
    aux[:, 8], aux[:, 9] = _lod(0.99), _lod(0.01)
    aux[:, 10] = np.float32(np.log(np.float64(1.0) / NF) - np.float64(aux[0, 4]))
    consts_d = nc.inline_tensor(aux, name="consts_t").ap()
    idxrow_d = nc.inline_tensor(np.arange(NTOK, dtype=np.float32)[None, :],
                                name="idxrow_t").ap()
    iotacol_d = nc.inline_tensor(np.arange(128, dtype=np.float32)[:, None],
                                 name="iotacol_t").ap()
    onesb_d = nc.inline_tensor(_bf(np.ones((1, FEAT))), name="onesb_t").ap()
    onescol_d = nc.inline_tensor(np.ones((128, 1), np.float32), name="onescol_t").ap()
    onesrow_d = nc.inline_tensor(np.ones((1, 128), np.float32), name="onesrow_t").ap()

    from contextlib import ExitStack
    es = ExitStack()
    sbuf = lambda name, shape, dt: es.enter_context(nc.sbuf_tensor(name, shape, dt))
    psumt = lambda name, shape: es.enter_context(nc.psum_tensor(name, shape, F32))
    mksem = lambda name: es.enter_context(nc.semaphore(name))

    ident = sbuf("ident", [128, 128], F32)
    identb = sbuf("identb", [128, 128], BF16)
    onescol = sbuf("onescol", [128, 1], F32)
    onesrow = sbuf("onesrow", [1, 128], F32)
    consts = sbuf("consts", [128, 12], F32)
    ccol = sbuf("ccol", [128, 1], F32)
    c1 = sbuf("c1", [64, K1H], BF16)
    s1 = sbuf("s1", [64, K1H], BF16)
    ca = sbuf("ca", [K1H, 64], BF16)
    sa = sbuf("sa", [K1H, 64], BF16)
    c1lo = sbuf("c1lo", [64, K1H], BF16)
    s1lo = sbuf("s1lo", [64, K1H], BF16)
    calo = sbuf("calo", [K1H, 64], BF16)
    salo = sbuf("salo", [K1H, 64], BF16)
    iotac = sbuf("iotac", [128, 1], F32)
    wo_sb = sbuf("wo_sb", [128, 8 * D], BF16)
    wcol = sbuf("wcol", [128, 8], F32)
    stg = sbuf("stg", [128, 4224], F32)
    stgb = sbuf("stgb", [128, 4096], BF16)
    stgb2 = sbuf("stgb2", [128, 2048], BF16)
    colbuf = sbuf("colbuf", [128, 44], F32)
    dcol_sb = sbuf("dcol_sb", [128, OTILE], F32)

    wsm = sbuf("wsm", [34, FEAT], F32)
    smallA = sbuf("smallA", [FEAT, FEAT], F32)
    smallB = sbuf("smallB", [FEAT, 20], F32)
    grow = sbuf("grow", [1, D], F32)
    growlo = sbuf("growlo", [1, D], F32)
    r127c = sbuf("r127c", [128, 1], F32)
    lng = sbuf("lng", [128, D], F32)
    robuf = sbuf("robuf", [FEAT, D], BF16)
    from contextlib import ExitStack as _ES
    es_ph = _ES()
    def sbuf_ph(name, shape, dt):
        return es_ph.enter_context(nc.sbuf_tensor(name, shape, dt))
    pb = [psumt(f"pb{i}", [128, 512]) for i in range(6)]
    pbb = [es.enter_context(nc.psum_tensor(f"pbb{i}", [128, 512], BF16)) for i in range(2)]
    sems_c = {e: mksem(f"sc_{e}") for e in ENGINES}
    sems_d = {e: mksem(f"sd_{e}") for e in ENGINES}
    s_fin = mksem("s_fin")

    l_c1 = p.add("gpsimd", lambda e: e.dma_start(out=c1[:], in_=c1_d), dma=True)
    l_s1 = p.add("gpsimd", lambda e: e.dma_start(out=s1[:], in_=s1_d), dma=True)
    l_ca = p.add("gpsimd", lambda e: e.dma_start(out=ca[:], in_=ca_d), dma=True)
    l_sa = p.add("gpsimd", lambda e: e.dma_start(out=sa[:], in_=sa_d), dma=True)
    l_c1lo = p.add("gpsimd", lambda e: e.dma_start(out=c1lo[:], in_=c1lo_d), dma=True)
    l_s1lo = p.add("gpsimd", lambda e: e.dma_start(out=s1lo[:], in_=s1lo_d), dma=True)
    l_calo = p.add("gpsimd", lambda e: e.dma_start(out=calo[:], in_=calo_d), dma=True)
    l_salo = p.add("gpsimd", lambda e: e.dma_start(out=salo[:], in_=salo_d), dma=True)
    mb_ops = []
    for src_d, dst_h, t_ in ((mb_d, mbb_hbm, 0), (mblo_d, mbb_hbm, 1),
                             (mbi_d, mbib_hbm, 0), (mbilo_d, mbib_hbm, 1)):
        for a0 in range(0, NPAIR, 8):
            na = min(8, NPAIR - a0)
            c_ = p.add("gpsimd", lambda e, src_d=src_d, a0=a0, na=na: e.dma_start(
                out=stgb[:, 0:na * 384].rearrange("p (a c) -> p a c", a=na),
                in_=src_d[a0:a0 + na].rearrange("a (p c) -> p a c", p=128)),
                deps=[mb_ops[-1]] if mb_ops else [], dma=True)
            st_ = p.add("sync", lambda e, dst_h=dst_h, a0=a0, na=na, t_=t_: e.dma_start(
                out=dst_h[a0:a0 + na, t_ * 49152:(t_ + 1) * 49152]
                .rearrange("a (p c) -> p a c", p=128),
                in_=stgb[:, 0:na * 384].rearrange("p (a c) -> p a c", a=na)),
                deps=[c_], dma=True)
            mb_ops.append(st_)
    p.barrier()
    l_id = p.add("sync", lambda e: e.dma_start(out=ident[:], in_=ident_d), dma=True)
    l_idb = p.add("sync", lambda e: e.dma_start(out=identb[:], in_=identb_d), dma=True)
    l_co = p.add("sync", lambda e: e.dma_start(out=consts[:], in_=consts_d), dma=True)
    l_io = p.add("sync", lambda e: e.dma_start(out=iotac[:], in_=iotacol_d), dma=True)
    l_oc = p.add("sync", lambda e: e.dma_start(out=onescol[:], in_=onescol_d), dma=True)
    l_or = p.add("sync", lambda e: e.dma_start(out=onesrow[:], in_=onesrow_d), dma=True)
    cbias = p.add("vector", lambda e: e.tensor_tensor(
        consts[:, 11:12], consts[:, 4:5], consts[:, 10:11], ALU.add),
        deps=[l_co], small=True)
    r1a = p.add("vector", lambda e: e.tensor_scalar(
        r127c[:, 0:1], consts[:, 7:8], 127.0, None, ALU.mult),
        deps=[l_co], small=True)
    r1b = p.add("vector", lambda e: e.reciprocal(r127c[:, 0:1], r127c[:, 0:1]),
                deps=[r1a], small=True)

    def psum_all(src_col, dst_col, deps):
        m1 = p.add("tensor", lambda e: e.matmul(pb[5][0:1, 0:1], src_col, onescol[:],
                                                start=True, stop=True),
                   deps=list(deps) + [l_oc])
        e1 = p.add("vector", lambda e: e.tensor_copy(colbuf[0:1, 42:43], pb[5][0:1, 0:1]),
                   deps=[m1], small=True)
        m2 = p.add("tensor", lambda e: e.matmul(pb[5][:, 1:2], onesrow[:],
                                                colbuf[0:1, 42:43], start=True, stop=True),
                   deps=[e1, l_or, m1])
        return p.add("vector", lambda e: e.tensor_copy(dst_col, pb[5][:, 1:2]),
                     deps=[m2], small=True)
    wo3 = wo.rearrange("(a q) d -> a q d", q=128)
    l_wo = [p.add("gpsimd", lambda e, kc=kc: e.dma_start(
        out=wo_sb[:, kc * D:(kc + 1) * D], in_=wo3[kc]), dma=True) for kc in range(8)]

    t1 = p.add("vector", lambda e: e.tensor_scalar(ccol[:], consts[:, 0:1], 0.5, None, ALU.add),
               deps=[l_co], small=True)
    t2 = p.add("vector", lambda e: e.tensor_tensor(ccol[:], ccol[:], consts[:, 0:1],
                                                   ALU.subtract), deps=[t1], small=True)
    l_cc = p.add("vector", lambda e: e.scalar_tensor_tensor(
        ccol[:], ccol[:], consts[:, 1:2], consts[:, 0:1], ALU.mult, ALU.add),
        deps=[t2], small=True)

    def quant3(dst_bf, buf, scol, clip_lo, clip_hi, deps):
        a = p.add("vector", lambda e: e.tensor_scalar(buf, buf, scol, ccol[:, 0:1],
                                                      ALU.mult, ALU.add), deps=list(deps) + [l_cc])
        b = p.add("vector", lambda e: e.tensor_scalar(buf, buf, ccol[:, 0:1], clip_lo,
                                                      ALU.subtract, ALU.max), deps=[a])
        return p.add("vector", lambda e: e.tensor_scalar(dst_bf, buf, clip_hi, None, ALU.min),
                     deps=[b])

    if PHASE_LIMIT >= 1 and 1 not in KSKIP:
        for w_in, wT, ntl, wid, wi in ((wg_in, wgT_hbm, 32, D, 0), (wu_in, wuT_hbm, 32, D, 1),
                                       (wd_in, wdT_hbm, 8, INNER, 2)):
            p.barrier()
            w3 = w_in.rearrange("(a q) c -> a q c", q=128)
            wT3 = wT.rearrange("(a q) c -> a q c", q=128)
            nblk = wid // 128
            nbuf = 2 if wid <= 1024 else 1
            abs_ops, prev_abs = [], {}
            for m in range(ntl):
                buf = stg[:, (m % nbuf) * 1024:(m % nbuf) * 1024 + wid]
                ld = p.add("sync", lambda e, b=buf, m=m, w3=w3: e.dma_start(out=b, in_=w3[m]),
                           deps=[prev_abs.get(m - nbuf)], dma=True)
                ab = p.add("scalar", lambda e, b=buf, m=m: e.activation(
                    b, b, ACT.Abs, accum_out=colbuf[:, m:m + 1]), deps=[ld])
                prev_abs[m] = ab
                abs_ops.append(ab)
            rs = p.add("vector", lambda e, ntl=ntl: e.reduce_sum(
                colbuf[:, 36:37], colbuf[:, 0:ntl], axis=AX), deps=abs_ops, small=True)
            pr = psum_all(colbuf[:, 36:37], colbuf[:, 37:38], [rs])
            sc1 = p.add("vector", lambda e, ntl=ntl, wid=wid, wi=wi: e.tensor_scalar(
                wcol[:, wi:wi + 1], colbuf[:, 37:38], 1.0 / (ntl * 128 * wid), 1e-5,
                ALU.mult, ALU.max), deps=[pr], small=True)
            sc2 = p.add("vector", lambda e, wi=wi: e.reciprocal(
                wcol[:, wi + 3:wi + 4], wcol[:, wi:wi + 1]), deps=[sc1], small=True)
            prev_fin = {}
            prev_lastr = {}
            tr_d, ev_d, dma_d = {}, {}, {}
            for m in range(ntl):
                buf = stg[:, (m % nbuf) * 1024:(m % nbuf) * 1024 + wid]
                ld = p.add("sync", lambda e, b=buf, m=m, w3=w3: e.dma_start(out=b, in_=w3[m]),
                           deps=[prev_fin.get(m - nbuf, sc2)], dma=True)
                q = quant3(stgb[:, 0:wid], buf, wcol[:, wi + 3:wi + 4], -1.0, 1.0,
                           [ld, sc2, prev_lastr.get(m - 1), mb_ops[-1]])
                last_dma = None
                for blk in range(nblk):
                    g = m * nblk + blk
                    bank = pbb[blk % 2]
                    tr = p.add("tensor", lambda e, bk=bank, blk=blk: e.transpose(
                        bk[:, 0:128], stgb[:, blk * 128:blk * 128 + 128], identb[:]),
                        deps=[q, l_idb, ev_d.get(g - 2)])
                    tr_d[g] = tr
                    prev_lastr[m] = tr
                    dst = stgb2[:, (g % 8) * 128:(g % 8) * 128 + 128]
                    ev = p.add("scalar", lambda e, bk=bank, d=dst: e.activation(
                        d, bk[:, 0:128], ACT.Copy), deps=[tr, dma_d.get(g - 8)])
                    ev_d[g] = ev
                    last_dma = p.add("gpsimd", lambda e, d=dst, blk=blk, m=m, wT3=wT3:
                                     e.dma_start(out=wT3[blk][:, m * 128:(m + 1) * 128],
                                                 in_=d), deps=[ev], dma=True)
                    dma_d[g] = last_dma
                prev_fin[m] = last_dma

    p.barrier()
    if PHASE_LIMIT >= 2 and 2 not in KSKIP:
        xnT = sbuf_ph("xnT", [128, 8 * NTOK], BF16)
        wuv_sb = sbuf_ph("wuv_sb", [128, 8 * 2048], BF16)
        wuv3 = wuv.rearrange("(a q) c -> a q c", q=128)
        l_wuv = [p.add("gpsimd", lambda e, kc=kc: e.dma_start(
            out=wuv_sb[:, kc * 2048:(kc + 1) * 2048], in_=wuv3[kc]), dma=True)
            for kc in range(8)]
        x3 = x_full.rearrange("(a q) c -> a q c", q=128)
        xo3 = x_own.rearrange("(a q) c -> a q c", q=128)
        v3 = v_hbm.rearrange("(a q) c -> a q c", q=128)
        u3 = u_hbm.rearrange("(a q) c -> a q c", q=128)
        xqT3 = xqT_hbm.rearrange("(a q) c -> a q c", q=128)

        def emit_ln_tile(xsrc, bufslot, deps):
            buf = stg[:, bufslot * 1024:bufslot * 1024 + 1024]
            ld = p.add("sync", lambda e, b=buf, xs=xsrc: e.dma_start(out=b, in_=xs),
                       deps=deps, dma=True)
            s1_ = p.add("vector", lambda e, b=buf: e.bn_stats(colbuf[:, 20:26], b[:, 0:512]),
                        deps=[ld], small=True)
            s2_ = p.add("vector", lambda e, b=buf: e.bn_stats(colbuf[:, 26:32], b[:, 512:1024]),
                        deps=[ld], small=True)
            ag = p.add("vector", lambda e: e.bn_aggr(colbuf[:, 8:10], colbuf[:, 20:32]),
                       deps=[s1_, s2_], small=True)
            sq = p.add("scalar", lambda e: e.activation(
                colbuf[:, 38:39], colbuf[:, 9:10], ACT.Sqrt, bias=consts[:, 2:3]),
                deps=[ag, l_co], small=True)
            rc = p.add("vector", lambda e: e.reciprocal(colbuf[:, 39:40], colbuf[:, 38:39]),
                       deps=[sq], small=True)
            nm = p.add("vector", lambda e, b=buf: e.tensor_scalar(
                b, b, colbuf[:, 8:9], colbuf[:, 39:40], ALU.subtract, ALU.mult),
                deps=[rc, ag])
            return nm, buf

        prev = {}
        for m in range(NTILE):
            nm, buf = emit_ln_tile(x3[m], m % 2, [prev.get(("vdma", m - 2))])
            evs = []
            for blk in range(8):
                bank = pb[2 + (blk % 2)]
                tr = p.add("tensor", lambda e, bk=bank, buf=buf, blk=blk: e.transpose(
                    bk[:, 0:128], buf[:, blk * 128:blk * 128 + 128], ident[:]),
                    deps=[nm, l_id, evs[blk - 2] if blk >= 2 else prev.get(("vmm", m - 1))])
                ev = p.add("vector", lambda e, bk=bank, blk=blk, m=m: e.tensor_copy(
                    xnT[:, blk * NTOK + m * 128: blk * NTOK + m * 128 + 128], bk[:, 0:128]),
                    deps=[tr])
                evs.append(ev)
            vevs = []
            for half in range(2):
                bank = pb[0 + half]
                mms = []
                for kc in range(8):
                    mm = p.add("tensor", lambda e, bk=bank, kc=kc, m=m, half=half: e.matmul(
                        bk[:], xnT[:, kc * NTOK + m * 128: kc * NTOK + m * 128 + 128],
                        wuv_sb[:, kc * 2048 + D + half * 512: kc * 2048 + D + half * 512 + 512],
                        start=(kc == 0), stop=(kc == 7)),
                        deps=[evs[kc], l_wuv[kc], prev.get(("vev", m - 1, half))])
                    mms.append(mm)
                ev = p.add("scalar", lambda e, bk=bank, half=half: e.activation(
                    stgb[:, 2048 + half * 512: 2048 + half * 512 + 512], bk[:], ACT.Silu),
                    deps=[mms[-1], prev.get(("vdma", m - 1))])
                prev[("vev", m, half)] = ev
                vevs.append(ev)
            prev[("vmm", m)] = vevs[-1]
            prev[("vdma", m)] = p.add("sync", lambda e, m=m: e.dma_start(
                out=v3[m], in_=stgb[:, 2048:2048 + 1024]), deps=vevs, dma=True)

        p.barrier()
        for m in range(OTILE):
            nm, buf = emit_ln_tile(xo3[m], 2 + (m % 2), [prev.get(("fin", m - 2))])
            evs = []
            for blk in range(8):
                bank = pb[2 + (blk % 2)]
                tr = p.add("tensor", lambda e, bk=bank, buf=buf, blk=blk: e.transpose(
                    bk[:, 0:128], buf[:, blk * 128:blk * 128 + 128], ident[:]),
                    deps=[nm, l_id, evs[blk - 2] if blk >= 2 else prev.get(("umm", m - 1))])
                ev = p.add("vector", lambda e, bk=bank, blk=blk: e.tensor_copy(
                    stgb2[:, blk * 128:blk * 128 + 128], bk[:, 0:128]),
                    deps=[tr, prev.get(("umm", m - 1))])
                evs.append(ev)
            uevs = []
            for half in range(2):
                bank = pb[0 + half]
                mms = []
                for kc in range(8):
                    mm = p.add("tensor", lambda e, bk=bank, kc=kc, half=half: e.matmul(
                        bk[:], stgb2[:, kc * 128:kc * 128 + 128],
                        wuv_sb[:, kc * 2048 + half * 512: kc * 2048 + half * 512 + 512],
                        start=(kc == 0), stop=(kc == 7)),
                        deps=[evs[kc], l_wuv[kc], prev.get(("uev", m - 1, half))])
                    mms.append(mm)
                ev = p.add("scalar", lambda e, bk=bank, half=half: e.activation(
                    stgb[:, 2048 + half * 512:2048 + half * 512 + 512], bk[:], ACT.Silu),
                    deps=[mms[-1], prev.get(("udma", m - 1))])
                prev[("uev", m, half)] = ev
                uevs.append(ev)
            prev[("umm", m)] = uevs[-1]
            prev[("udma", m)] = p.add("sync", lambda e, m=m: e.dma_start(
                out=u3[m], in_=stgb[:, 2048:2048 + 1024]), deps=uevs, dma=True)
            rm = p.add("vector", lambda e, buf=buf: e.reduce_max(
                colbuf[:, 32:33], buf, axis=AX, apply_absolute_value=True),
                deps=[nm], small=True)
            mx = p.add("vector", lambda e: e.tensor_scalar(
                colbuf[:, 33:34], colbuf[:, 32:33], 1e-5, None, ALU.max),
                deps=[rm], small=True)
            rc = p.add("vector", lambda e: e.reciprocal(colbuf[:, 34:35], colbuf[:, 33:34]),
                       deps=[mx], small=True)
            sc = p.add("vector", lambda e: e.tensor_scalar(
                colbuf[:, 35:36], colbuf[:, 34:35], 127.0, None, ALU.mult),
                deps=[rc], small=True)
            p.add("vector", lambda e, m=m: e.tensor_scalar(
                dcol_sb[:, m:m + 1], colbuf[:, 33:34], r127c[:, 0:1], None, ALU.mult),
                deps=[mx, r1b], small=True)
            q = quant3(stgb[:, 0:1024], buf, colbuf[:, 35:36], -128.0, 127.0,
                       [sc, nm, prev.get(("xqtr", m - 1))])
            lastq = None
            prev_xev = {}
            for blk in range(8):
                bank = pbb[blk % 2]
                tr = p.add("tensor", lambda e, bk=bank, blk=blk: e.transpose(
                    bk[:, 128:256], stgb[:, blk * 128:blk * 128 + 128], identb[:]),
                    deps=[q, l_idb, evs[blk], prev_xev.get(blk - 2)])
                ev = p.add("vector", lambda e, bk=bank, blk=blk: e.tensor_copy(
                    stgb[:, 1024 + blk * 128:1024 + blk * 128 + 128], bk[:, 128:256]),
                    deps=[tr])
                prev_xev[blk] = ev
                prev[("xqtr", m)] = tr
                lastq = p.add("sync", lambda e, blk=blk, m=m: e.dma_start(
                    out=xqT3[blk][:, m * 128:(m + 1) * 128],
                    in_=stgb[:, 1024 + blk * 128:1024 + blk * 128 + 128]),
                    deps=[ev], dma=True)
            prev[("fin", m)] = lastq


    if PHASE_LIMIT >= 3 and 3 not in KSKIP:
        p.barrier()
        KS3 = os.environ.get("KSERIAL3", "")
        if "x" in KS3:
            p.serial = True
        es_ph.close(); es_ph = _ES()
        hbuf = sbuf_ph("hbuf", [34, NTOK], F32)
        hbuf2 = sbuf_ph("hbuf2", [34, NTOK], F32)
        rowbuf = sbuf_ph("rowbuf", [1, NTOK], F32)
        rowbuf2 = sbuf_ph("rowbuf2", [1, NTOK], F32)
        t3h = t_hbm.rearrange("(a q) c -> a q c", q=128)
        l_ix = p.add("sync", lambda e: e.dma_start(out=rowbuf[:], in_=idxrow_d), dma=True)
        l_ri = p.add("sync", lambda e: e.dma_start(out=smallB[0:1, 0:8].broadcast_to([1, 8])
                                                   if False else smallA[0:1, 0:FEAT],
                                                   in_=rpe_in), dma=True)
        riT = p.add("gpsimd", lambda e: e.dma_start(out=smallB[0:FEAT, 0:1],
                                                    in_=smallA[0:1, 0:FEAT]),
                    deps=[l_ri], dma=True)
        h_ready = []
        for sl in range(8):
            bank = pb[sl % 2]
            mm = p.add("tensor", lambda e, bk=bank, sl=sl: e.matmul(
                bk[0:FEAT, 0:512], smallA[0:1, 0:FEAT], rowbuf[0:1, sl * 512:(sl + 1) * 512],
                start=True, stop=True),
                deps=[l_ri, l_ix] + ([h_ready[sl - 2]] if sl >= 2 else []))
            ev = p.add("scalar", lambda e, bk=bank, sl=sl: e.activation(
                hbuf[0:FEAT, sl * 512:(sl + 1) * 512], bk[0:FEAT, 0:512], ACT.Silu),
                deps=[mm])
            h_ready.append(ev)
        hcur, hnxt = hbuf, hbuf2
        pl = {}
        for li in range(int(os.environ.get("KRPE", RPE_L))):
            if "s" in KS3:
                p.serial = True
            sq = p.add("scalar", lambda e, hc=hcur, hn=hnxt: e.activation(
                hn[0:FEAT, :], hc[0:FEAT, :], ACT.Square), deps=h_ready)
            sc32 = p.add("vector", lambda e: e.tensor_scalar(
                smallB[0:FEAT, 16:17], onescol[0:FEAT, 0:1], 1.0 / FEAT, None, ALU.mult),
                deps=[l_oc], small=True)
            mu_evs, sqe = [], []
            for sl in range(8):
                bank = pb[2 + sl % 2]
                mm = p.add("tensor", lambda e, bk=bank, sl=sl, hc=hcur: e.matmul(
                    bk[0:1, 0:512], smallB[0:FEAT, 16:17], hc[0:FEAT, sl * 512:(sl + 1) * 512],
                    start=True, stop=True),
                    deps=[sc32] + h_ready + ([mu_evs[sl - 2]] if sl >= 2 else []))
                ev = p.add("vector", lambda e, bk=bank, sl=sl: e.tensor_copy(
                    rowbuf[0:1, sl * 512:(sl + 1) * 512], bk[0:1, 0:512]),
                    deps=[mm], small=True)
                mu_evs.append(ev)
            for sl in range(8):
                bank = pb[4 + sl % 2]
                mm = p.add("tensor", lambda e, bk=bank, sl=sl, hn=hnxt: e.matmul(
                    bk[0:1, 0:512], smallB[0:FEAT, 16:17], hn[0:FEAT, sl * 512:(sl + 1) * 512],
                    start=True, stop=True),
                    deps=[sc32, sq] + ([sqe[sl - 2]] if sl >= 2 else []))
                ev = p.add("vector", lambda e, bk=bank, sl=sl: e.tensor_copy(
                    rowbuf2[0:1, sl * 512:(sl + 1) * 512], bk[0:1, 0:512]),
                    deps=[mm], small=True)
                sqe.append(ev)
            pk1 = p.add("gpsimd", lambda e: e.dma_start(out=stg[:, 4096:4128],
                                                        in_=rowbuf[0:1, :]),
                        deps=mu_evs, dma=True)
            pk2 = p.add("gpsimd", lambda e: e.dma_start(out=stg[:, 4128:4160],
                                                        in_=rowbuf2[0:1, :]),
                        deps=sqe, dma=True)
            vv = p.add("vector", lambda e: e.tensor_tensor(
                stg[:, 4160:4192], stg[:, 4096:4128], stg[:, 4096:4128], ALU.mult),
                deps=[pk1], small=True)
            vv2 = p.add("vector", lambda e: e.tensor_tensor(
                stg[:, 4160:4192], stg[:, 4128:4160], stg[:, 4160:4192], ALU.subtract),
                deps=[vv, pk2], small=True)
            sdq = p.add("scalar", lambda e: e.activation(
                stg[:, 4160:4192], stg[:, 4160:4192], ACT.Sqrt, bias=consts[:, 2:3]),
                deps=[vv2, l_co], small=True)
            rcq = p.add("vector", lambda e: e.reciprocal(stg[:, 4192:4224],
                                                         stg[:, 4160:4192]),
                        deps=[sdq], small=True)
            upk = p.add("gpsimd", lambda e: e.dma_start(out=rowbuf2[0:1, :],
                                                        in_=stg[:, 4192:4224]),
                        deps=[rcq], dma=True)
            ng = p.add("vector", lambda e, hc=hcur: e.tensor_scalar(
                hc[FEAT:FEAT + 1, :], rowbuf[0:1, :], -1.0, None, ALU.mult),
                deps=[pk1] + h_ready, small=True)
            if "h_dbg" in DBG and os.environ.get("KROWDBG"):
                p.add("sync", lambda e: e.dma_start(out=h_dbg[32:33, :], in_=rowbuf[0:1, :]),
                      deps=[ng], dma=True)
                p.add("sync", lambda e: e.dma_start(out=h_dbg[33:34, :], in_=rowbuf2[0:1, :]),
                      deps=[upk], dma=True)
            if "s" in KS3 and "f" not in KS3:
                p.serial = False
            if "f" in KS3:
                p.serial = True
            l_wh = p.add("sync", lambda e, li=li: e.dma_start(
                out=smallA[:], in_=rpe_hid[li * FEAT:(li + 1) * FEAT, :]),
                deps=([riT] if li == 0 else [pl.get("wgp"), pl.get("rmm")]) + h_ready,
                dma=True)
            l_g = p.add("sync", lambda e, li=li: e.dma_start(
                out=grow[0:1, 0:FEAT], in_=rpe_g[li:li + 1, :]),
                deps=[pl.get("gc")], dma=True)
            l_b = p.add("sync", lambda e, li=li: e.dma_start(
                out=grow[0:1, FEAT:2 * FEAT], in_=rpe_b[li:li + 1, :]),
                deps=[pl.get("bc")], dma=True)
            gc = p.add("gpsimd", lambda e: e.dma_start(out=smallB[0:FEAT, 4:5],
                                                       in_=grow[0:1, 0:FEAT]),
                       deps=[l_g, pl.get("wgp")], dma=True)
            bc = p.add("gpsimd", lambda e: e.dma_start(out=smallB[0:FEAT, 8:9],
                                                       in_=grow[0:1, FEAT:2 * FEAT]),
                       deps=[l_b, pl.get("rmm")], dma=True)
            rmm = p.add("tensor", lambda e: e.matmul(
                pb[0][0:1, 0:FEAT], smallB[0:FEAT, 8:9], smallA[:],
                start=True, stop=True), deps=[bc, l_wh, pl.get("rev")] + h_ready)
            rev = p.add("vector", lambda e: e.tensor_copy(grow[0:1, 2 * FEAT:3 * FEAT],
                                                          pb[0][0:1, 0:FEAT]),
                        deps=[rmm], small=True)
            rcl = p.add("gpsimd", lambda e: e.dma_start(out=smallB[0:FEAT, 12:13],
                                                        in_=grow[0:1, 2 * FEAT:3 * FEAT]),
                        deps=[rev] + h_ready, dma=True)
            wgp = p.add("vector", lambda e: e.tensor_scalar(
                wsm[0:FEAT, :], smallA[:], smallB[0:FEAT, 4:5], None, ALU.mult),
                deps=[gc, l_wh, rmm] + h_ready, small=True)
            qmm = p.add("tensor", lambda e: e.matmul(
                pb[1][0:1, 0:FEAT], onescol[0:FEAT, 0:1], wsm[0:FEAT, :],
                start=True, stop=True), deps=[wgp, l_oc, pl.get("qev")] + h_ready)
            qev = p.add("vector", lambda e: e.tensor_copy(wsm[FEAT:FEAT + 1, :],
                                                          pb[1][0:1, 0:FEAT]),
                        deps=[qmm], small=True)
            pl = dict(gc=gc, bc=bc, rcl=rcl, wgp=wgp, rmm=rmm, qev=qev, rev=rev)
            if "f" in KS3 and "l" not in KS3:
                p.serial = False
            if "l" in KS3:
                p.serial = True
            newh = []
            for sl in range(8):
                b1 = pb[2 + sl % 2]
                b2 = pb[4 + sl % 2]
                mm1 = p.add("tensor", lambda e, bk=b1, sl=sl, hc=hcur: e.matmul(
                    bk[0:FEAT, 0:512], wsm[0:FEAT + 1, :],
                    hc[0:FEAT + 1, sl * 512:(sl + 1) * 512], start=True, stop=True),
                    deps=[qev, ng] + h_ready + ([newh[sl - 2]] if sl >= 2 else []))
                mmi = p.add("tensor", lambda e, bk=b2, sl=sl: e.matmul(
                    bk[0:FEAT, 0:512], onesrow[0:1, 0:FEAT],
                    rowbuf2[0:1, sl * 512:(sl + 1) * 512], start=True, stop=True),
                    deps=[upk, l_or] + ([newh[sl - 2]] if sl >= 2 else []))
                iev = p.add("vector", lambda e, bk=b2, sl=sl: e.tensor_copy(
                    stg[0:FEAT, 2048 + (sl % 2) * 512:2048 + (sl % 2) * 512 + 512],
                    bk[0:FEAT, 0:512]), deps=[mmi] + ([newh[sl - 2]] if sl >= 2 else []))
                tt = p.add("vector", lambda e, bk=b1, sl=sl, hn=hnxt: e.tensor_tensor(
                    hn[0:FEAT, sl * 512:(sl + 1) * 512], bk[0:FEAT, 0:512],
                    stg[0:FEAT, 2048 + (sl % 2) * 512:2048 + (sl % 2) * 512 + 512],
                    ALU.mult), deps=[mm1, iev, sq])
                sv = p.add("scalar", lambda e, sl=sl, hn=hnxt: e.activation(
                    hn[0:FEAT, sl * 512:(sl + 1) * 512],
                    hn[0:FEAT, sl * 512:(sl + 1) * 512], ACT.Silu,
                    bias=smallB[0:FEAT, 12:13]), deps=[tt, rcl])
                newh.append(sv)
            if KS3 and "x" not in KS3:
                p.serial = False
            h_ready = newh
            hcur, hnxt = hnxt, hcur
        if "h_dbg" in DBG:
            p.add("sync", lambda e, hc=hcur: e.dma_start(out=h_dbg[0:FEAT, :],
                                                         in_=hc[0:FEAT, :]),
                  deps=h_ready, dma=True)
        hb_ops = [p.add("vector", lambda e, sl=sl, hc=hcur: e.tensor_copy(
            stgb[0:FEAT, sl * 512:(sl + 1) * 512], hc[0:FEAT, sl * 512:(sl + 1) * 512]),
            deps=h_ready) for sl in range(8)]
        l_ro = p.add("gpsimd", lambda e: e.dma_start(out=robuf[:], in_=rpe_out),
                     deps=h_ready, dma=True)
        l_dg = p.add("sync", lambda e: e.dma_start(out=grow[0:1, :], in_=dgamma),
                     deps=h_ready, dma=True)
        sg = p.add("scalar", lambda e: e.activation(grow[0:1, :], grow[0:1, :], ACT.Sigmoid),
                   deps=[l_dg], small=True)
        gmlo = p.add("vector", lambda e: e.tensor_scalar(
            growlo[0:1, :], grow[0:1, :], consts[0:1, 9:10], consts[0:1, 8:9],
            ALU.mult, ALU.add), deps=[sg, l_co], small=True)
        gm = p.add("vector", lambda e: e.tensor_scalar(
            grow[0:1, :], grow[0:1, :], consts[0:1, 6:7], consts[0:1, 5:6],
            ALU.mult, ALU.add), deps=[sg, l_co, gmlo], small=True)
        gms = p.add("vector", lambda e: e.tensor_tensor(
            grow[0:1, :], grow[0:1, :], growlo[0:1, :], ALU.add),
            deps=[gm, gmlo], small=True)
        lnr = p.add("scalar", lambda e: e.activation(grow[0:1, :], grow[0:1, :], ACT.Ln),
                    deps=[gms], small=True)
        lng_evs = []
        for half in range(2):
            bank = pb[2 + half]
            mm = p.add("tensor", lambda e, bk=bank, half=half: e.matmul(
                bk[:], onesrow[0:1, :], grow[0:1, half * 512:(half + 1) * 512],
                start=True, stop=True), deps=[lnr, l_or] + h_ready)
            ev = p.add("vector", lambda e, bk=bank, half=half: e.tensor_copy(
                lng[:, half * 512:(half + 1) * 512], bk[:]), deps=[mm])
            lng_evs.append(ev)
        tprev = {}
        for m in range(NTILE):
            kc = p.add("vector", lambda e, m=m: e.tensor_scalar(
                colbuf[:, 40:41], iotac[:, 0:1], float(128 * m), None, ALU.add),
                deps=[l_io, tprev.get(("eg", m - 1))], small=True)
            eg = p.add("scalar", lambda e: e.activation(
                stg[:, 1024:2048], lng[:], ACT.Exp, bias=consts[:, 11:12],
                scale=colbuf[:, 40:41]), deps=[kc, l_co, cbias] + lng_evs)
            tprev[("eg", m)] = eg
            tevs = []
            for half in range(2):
                bank = pb[half]
                mm = p.add("tensor", lambda e, bk=bank, m=m, half=half: e.matmul(
                    bk[:], stgb[0:FEAT, m * 128:(m + 1) * 128],
                    robuf[0:FEAT, half * 512:(half + 1) * 512],
                    start=True, stop=True),
                    deps=hb_ops + [l_ro, tprev.get(("tt", m - 1, half))])
                tt = p.add("vector", lambda e, bk=bank, m=m, half=half: e.tensor_tensor(
                    stgb[FEAT:FEAT + 64, 3072 + half * 512:3072 + half * 512 + 512]
                    if False else stgb2[:, half * 512:half * 512 + 512],
                    bk[:], stg[:, 1024 + half * 512:1024 + half * 512 + 512], ALU.mult),
                    deps=[mm, eg, tprev.get(("tdma", m - 1))])
                tprev[("tt", m, half)] = tt
                tevs.append(tt)
            tprev[("tdma", m)] = p.add("sync", lambda e, m=m: e.dma_start(
                out=t3h[m], in_=stgb2[:, 0:1024]), deps=tevs, dma=True)

    p.serial = False
    if PHASE_LIMIT >= 4 and 4 not in KSKIP:
        p.barrier()
        es_ph.close(); es_ph = _ES()
        tcbuf = sbuf_ph("tcbuf", [64, N2 * CCH], BF16)
        that = sbuf_ph("that", [128, NPAIR * 2 * CCH], BF16)
        bmov = sbuf_ph("bmov", [128, 8 * 2 * CCH], BF16)
        amov = sbuf_ph("amov", [K1H, 2 * 8 * CCH], BF16)
        ostg = sbuf_ph("ostg", [64, N2 * CCH], BF16)
        mbbuf = sbuf_ph("mbbuf", [128, 2 * 6 * 128], BF16)
        mbibuf = sbuf_ph("mbibuf", [128, 2 * 6 * 128], BF16)
        astg = sbuf_ph("astg", [K1H, 2048], BF16)
        ROWB = N2 * 2 * CCH
        GRPS = [(0, 8), (8, 8), (16, 8), (24, 8), (32, 1)]
        A4 = A_hbm.rearrange("k (j r c) -> k j r c", r=2, c=CCH)
        E4 = E_hbm.rearrange("k (j r c) -> k j r c", r=2, c=CCH)
        zA0 = p.add("vector", lambda e: e.memset(stgb[:, 0:256], 0.0))
        zA = p.add("sync", lambda e: e.dma_start(out=A_hbm[65:66, :], in_=stgb[:, 0:256]),
                   deps=[zA0], dma=True)
        pv = {}
        for cc in range(NCHUNK):
            for sig, sig_hbm in (("t", t_hbm), ("v", v_hbm)):
                sg4 = sig_hbm.rearrange("(j1 j2) (cb c) -> j1 j2 cb c", j2=N2, c=CCH)
                t0 = p.add("sync", lambda e, sg4=sg4, cc=cc: e.dma_start(
                    out=tcbuf[:], in_=sg4[:, :, cc, :]),
                    deps=[pv.get("lastAmm"), pv.get(("odma", cc - 1))]
                    + pv.get("t1prev", []), dma=True)
                adma = {}
                for g2 in range(32):
                    pA = pb[(g2 % 2) * 2]
                    pB = pb[(g2 % 2) * 2 + 1]
                    stslot = astg[0:K1H, (g2 % 2) * 1024:(g2 % 2) * 1024 + 1024]
                    st4 = stslot.rearrange("p (j r c) -> p j r c", j=2, r=2, c=CCH)
                    mmc = p.add("tensor", lambda e, pA=pA, g2=g2: e.matmul(
                        pA[0:K1H, 0:512], c1[:], tcbuf[:, g2 * 512:(g2 + 1) * 512],
                        start=True, stop=False),
                        deps=[t0, l_c1, adma.get(g2 - 2)])
                    mmc2 = p.add("tensor", lambda e, pA=pA, g2=g2: e.matmul(
                        pA[0:K1H, 0:512], c1lo[:], tcbuf[:, g2 * 512:(g2 + 1) * 512],
                        start=False, stop=True), deps=[mmc, l_c1lo])
                    mms = p.add("tensor", lambda e, pB=pB, g2=g2: e.matmul(
                        pB[0:K1H, 0:512], s1[:], tcbuf[:, g2 * 512:(g2 + 1) * 512],
                        start=True, stop=False),
                        deps=[t0, l_s1, adma.get(g2 - 2)])
                    mms2 = p.add("tensor", lambda e, pB=pB, g2=g2: e.matmul(
                        pB[0:K1H, 0:512], s1lo[:], tcbuf[:, g2 * 512:(g2 + 1) * 512],
                        start=False, stop=True), deps=[mms, l_s1lo])
                    evr = p.add("vector", lambda e, pA=pA, st4=st4: e.tensor_copy(
                        st4[:, :, 0, :],
                        pA[0:K1H, 0:512].rearrange("p (j c) -> p j c", c=CCH)),
                        deps=[mmc2, adma.get(g2 - 2)])
                    evi = p.add("vector", lambda e, pB=pB, st4=st4: e.tensor_copy(
                        st4[:, :, 1, :],
                        pB[0:K1H, 0:512].rearrange("p (j c) -> p j c", c=CCH)),
                        deps=[mms2, adma.get(g2 - 2)])
                    adma[g2] = p.add("scalar", lambda e, st=stslot, g2=g2: e.dma_start(
                        out=A_hbm[0:K1H, g2 * 1024:(g2 + 1) * 1024], in_=st),
                        deps=[evr, evi, zA], dma=True)
                pv["lastAmm"] = adma[31]
                lastA = [adma[g] for g in range(32)]
                for gi, (p0, np_) in enumerate(GRPS):
                    t1d = []
                    for hh in range(2):
                        dst = bmov[64 * hh:64 * hh + 64, 0:np_ * 512].rearrange(
                            "j (pp r c) -> j pp r c", r=2, c=CCH)
                        t1d.append(p.add("sync", lambda e, dst=dst, p0=p0, hh=hh, np_=np_:
                                         e.dma_start(out=dst, in_=A_hbm.rearrange(
                                             "(kk) (j r c) -> kk j r c", r=2, c=CCH)
                                             [2 * p0 + hh: 2 * (p0 + np_ - 1) + hh + 1: 2]
                                             .rearrange("pp j r c -> j pp r c")),
                                         deps=lastA + [pv.get("bmmlast")], dma=True))
                    pv.setdefault("t1acc", []).extend(t1d)
                    for pi in range(np_):
                        pp = p0 + pi
                        mslot = mbbuf[:, (pp % 2) * 768:(pp % 2) * 768 + 768]
                        lt = p.add("scalar", lambda e, ms=mslot, pp=pp: e.dma_start(
                            out=ms.rearrange("p (t c) -> p t c", t=2),
                            in_=mbb_hbm[pp].rearrange("(t p c) -> p t c", p=128, c=384)),
                            deps=[pv.get(("mbslot", pp % 2))] + mb_ops, dma=True)
                        bXr = pb[4]
                        bXi = pb[5]
                        mr = lambda ms=mslot: ms[:, 0:128]
                        mi = lambda ms=mslot: ms[:, 128:256]
                        mn = lambda ms=mslot: ms[:, 256:384]
                        dep_ev = [pv.get(("bev", sig, cc, pp - 1))]
                        m1 = p.add("tensor", lambda e, ms=mslot, pi=pi: e.matmul(
                            pb[4][:, 0:CCH], ms[:, 0:128],
                            bmov[:, pi * 512:pi * 512 + 256], start=True, stop=False),
                            deps=t1d + [lt] + dep_ev)
                        m1b = p.add("tensor", lambda e, ms=mslot, pi=pi: e.matmul(
                            pb[4][:, 0:CCH], ms[:, 384:512],
                            bmov[:, pi * 512:pi * 512 + 256], start=False, stop=False),
                            deps=[m1])
                        m2 = p.add("tensor", lambda e, ms=mslot, pi=pi: e.matmul(
                            pb[4][:, 0:CCH], ms[:, 256:384],
                            bmov[:, pi * 512 + 256:pi * 512 + 512], start=False, stop=False),
                            deps=[m1b])
                        m2b = p.add("tensor", lambda e, ms=mslot, pi=pi: e.matmul(
                            pb[4][:, 0:CCH], ms[:, 640:768],
                            bmov[:, pi * 512 + 256:pi * 512 + 512], start=False, stop=True),
                            deps=[m2])
                        m3 = p.add("tensor", lambda e, ms=mslot, pi=pi: e.matmul(
                            pb[5][:, 0:CCH], ms[:, 128:256],
                            bmov[:, pi * 512:pi * 512 + 256], start=True, stop=False),
                            deps=[m2b])
                        m3b = p.add("tensor", lambda e, ms=mslot, pi=pi: e.matmul(
                            pb[5][:, 0:CCH], ms[:, 512:640],
                            bmov[:, pi * 512:pi * 512 + 256], start=False, stop=False),
                            deps=[m3])
                        m4 = p.add("tensor", lambda e, ms=mslot, pi=pi: e.matmul(
                            pb[5][:, 0:CCH], ms[:, 0:128],
                            bmov[:, pi * 512 + 256:pi * 512 + 512], start=False, stop=False),
                            deps=[m3b])
                        m4b = p.add("tensor", lambda e, ms=mslot, pi=pi: e.matmul(
                            pb[5][:, 0:CCH], ms[:, 384:512],
                            bmov[:, pi * 512 + 256:pi * 512 + 512], start=False, stop=True),
                            deps=[m4])
                        pv[("mbslot", pp % 2)] = m4b
                        pv["bmmlast"] = m4b
                        if sig == "t":
                            ev = p.add("vector", lambda e, pp=pp: e.tensor_copy(
                                that.rearrange("q (pz r c) -> q pz r c", r=2, c=CCH)
                                [:, pp, 0, :], pb[4][:, 0:CCH]),
                                deps=[m2b] + pv.get(("spuse", pp), []))
                            ev2 = p.add("vector", lambda e, pp=pp: e.tensor_copy(
                                that.rearrange("q (pz r c) -> q pz r c", r=2, c=CCH)
                                [:, pp, 1, :], pb[5][:, 0:CCH]),
                                deps=[m4b] + pv.get(("spuse", pp), []))
                            pv[("bev", sig, cc, pp)] = ev2
                        else:
                            exr = p.add("vector", lambda e: e.tensor_copy(
                                stgb2[:, 0:CCH], pb[4][:, 0:CCH]),
                                deps=[m2b] + pv.get("spect_done", []))
                            exi = p.add("vector", lambda e: e.tensor_copy(
                                stgb2[:, CCH:2 * CCH], pb[5][:, 0:CCH]),
                                deps=[m4b] + pv.get("spect_done", []))
                            pv[("bev", sig, cc, pp)] = exi
                            th4 = that.rearrange("q (pz r c) -> q pz r c", r=2, c=CCH)
                            sm1 = p.add("vector", lambda e, pp=pp: e.tensor_tensor(
                                stgb2[:, 512:512 + CCH], stgb2[:, 0:CCH],
                                th4[:, pp, 0, :], ALU.mult), deps=[exr])
                            sm2 = p.add("vector", lambda e, pp=pp: e.tensor_tensor(
                                stgb2[:, 768:768 + CCH], stgb2[:, CCH:2 * CCH],
                                th4[:, pp, 1, :], ALU.mult), deps=[exi])
                            spr = p.add("vector", lambda e: e.tensor_tensor(
                                stgb2[:, 1024:1024 + CCH], stgb2[:, 512:512 + CCH],
                                stgb2[:, 768:768 + CCH], ALU.subtract), deps=[sm1, sm2])
                            sm3 = p.add("gpsimd", lambda e, pp=pp: e.tensor_tensor(
                                stgb2[:, 512:512 + CCH], stgb2[:, 0:CCH],
                                th4[:, pp, 1, :], ALU.mult), deps=[exr, spr])
                            sm4 = p.add("gpsimd", lambda e, pp=pp: e.tensor_tensor(
                                stgb2[:, 768:768 + CCH], stgb2[:, CCH:2 * CCH],
                                th4[:, pp, 0, :], ALU.mult), deps=[exi, spr])
                            spi = p.add("vector", lambda e: e.tensor_tensor(
                                stgb2[:, 1280:1280 + CCH], stgb2[:, 512:512 + CCH],
                                stgb2[:, 768:768 + CCH], ALU.add), deps=[sm3, sm4])
                            pv[("spuse", pp)] = [sm1, sm2, sm3, sm4]
                            mslot2 = mbibuf[:, (pp % 2) * 768:(pp % 2) * 768 + 768]
                            lt2 = p.add("scalar", lambda e, ms=mslot2, pp=pp: e.dma_start(
                                out=ms.rearrange("p (t c) -> p t c", t=2),
                                in_=mbib_hbm[pp].rearrange("(t p c) -> p t c", p=128, c=384)),
                                deps=[pv.get(("mbislot", pp % 2))] + mb_ops, dma=True)
                            e1 = p.add("tensor", lambda e, ms=mslot2: e.matmul(
                                pb[2][:, 0:CCH], ms[:, 0:128], stgb2[:, 1024:1024 + CCH],
                                start=True, stop=False),
                                deps=[spr, lt2, pv.get(("eev", cc, pp - 1))])
                            e1b = p.add("tensor", lambda e, ms=mslot2: e.matmul(
                                pb[2][:, 0:CCH], ms[:, 384:512], stgb2[:, 1024:1024 + CCH],
                                start=False, stop=False), deps=[e1])
                            e2 = p.add("tensor", lambda e, ms=mslot2: e.matmul(
                                pb[2][:, 0:CCH], ms[:, 256:384], stgb2[:, 1280:1280 + CCH],
                                start=False, stop=False), deps=[e1b, spi])
                            e2b = p.add("tensor", lambda e, ms=mslot2: e.matmul(
                                pb[2][:, 0:CCH], ms[:, 640:768], stgb2[:, 1280:1280 + CCH],
                                start=False, stop=True), deps=[e2])
                            pv["spect_done"] = [spr, spi, e2b]
                            e3 = p.add("tensor", lambda e, ms=mslot2: e.matmul(
                                pb[3][:, 0:CCH], ms[:, 128:256], stgb2[:, 1024:1024 + CCH],
                                start=True, stop=False), deps=[e2b])
                            e3b = p.add("tensor", lambda e, ms=mslot2: e.matmul(
                                pb[3][:, 0:CCH], ms[:, 512:640], stgb2[:, 1024:1024 + CCH],
                                start=False, stop=False), deps=[e3])
                            e4 = p.add("tensor", lambda e, ms=mslot2: e.matmul(
                                pb[3][:, 0:CCH], ms[:, 0:128], stgb2[:, 1280:1280 + CCH],
                                start=False, stop=False), deps=[e3b])
                            e4b = p.add("tensor", lambda e, ms=mslot2: e.matmul(
                                pb[3][:, 0:CCH], ms[:, 384:512], stgb2[:, 1280:1280 + CCH],
                                start=False, stop=True), deps=[e4])
                            pv[("mbislot", pp % 2)] = e4b
                            pv["spect_done"] = [e2b, e4b]
                            sb4 = stgb.rearrange("q (pz r c) -> q pz r c", r=2, c=CCH)
                            ee1 = p.add("vector", lambda e, pi=pi: e.tensor_copy(
                                sb4[:, pi, 0, :], pb[2][:, 0:CCH]),
                                deps=[e2b] + pv.get("eprev_dma", []))
                            ee2 = p.add("vector", lambda e, pi=pi: e.tensor_copy(
                                sb4[:, pi, 1, :], pb[3][:, 0:CCH]),
                                deps=[e4b] + pv.get("eprev_dma", []))
                            pv[("eev", cc, pp)] = ee2
                            pv[("eevs", gi, pi)] = (ee1, ee2)
                    if sig == "v":
                        pass
                        edeps = [x for pi2 in range(np_) for x in pv[("eevs", gi, pi2)]]
                        for hh in range(2):
                            src = stgb[64 * hh:64 * hh + 64, 0:np_ * 512].rearrange(
                                "j (pp r c) -> j pp r c", r=2, c=CCH)
                            pv[("edma", gi, hh)] = p.add("sync", lambda e, src=src,
                                                         p0=p0, hh=hh, np_=np_:
                                                         e.dma_start(
                                out=E_hbm.rearrange("(kk) (j r c) -> kk j r c", r=2, c=CCH)
                                [2 * p0 + hh: 2 * (p0 + np_ - 1) + hh + 1: 2]
                                .rearrange("pp j r c -> j pp r c"), in_=src),
                                deps=edeps + pv.get("eaprev", []), dma=True)
                        pv["eprev_dma"] = [pv[("edma", gi, 0)], pv[("edma", gi, 1)]]
                pv["t1prev"] = pv.pop("t1acc", [])
            if "that_dbg" in DBG:
                p.add("sync", lambda e: e.dma_start(out=that_dbg, in_=that[:]),
                      deps=[pv[("bev", "t", cc, pp2)] for pp2 in range(NPAIR)], dma=True)
            edall = [pv[("edma", gi, hh)] for gi in range(5) for hh in range(2)
                     if ("edma", gi, hh) in pv]
            for jg in range(8):
                a1 = p.add("sync", lambda e, jg=jg: e.dma_start(
                    out=amov[:, 0:8 * CCH].rearrange("k (j c) -> k j c", c=CCH),
                    in_=E4[0:K1H, jg * 8:(jg + 1) * 8, 0, :]),
                    deps=edall + [pv.get(("apev", jg - 1))], dma=True)
                a2 = p.add("sync", lambda e, jg=jg: e.dma_start(
                    out=amov[:, 8 * CCH:16 * CCH].rearrange("k (j c) -> k j c", c=CCH),
                    in_=E4[0:K1H, jg * 8:(jg + 1) * 8, 1, :]),
                    deps=edall + [pv.get(("apev", jg - 1))], dma=True)
                pv.setdefault("eacc", []).extend([a1, a2])
                for pr in range(4):
                    bnk = pb[pr % 2]
                    am1 = p.add("tensor", lambda e, bnk=bnk, pr=pr: e.matmul(
                        bnk[0:64, 0:512], ca[:], amov[:, pr * 512:(pr + 1) * 512],
                        start=True, stop=False),
                        deps=[a1, l_ca, pv.get(("apev2", jg, pr - 2)) if pr >= 2
                              else pv.get(("apev2", jg - 1, pr + 2))])
                    am1b = p.add("tensor", lambda e, bnk=bnk, pr=pr: e.matmul(
                        bnk[0:64, 0:512], calo[:], amov[:, pr * 512:(pr + 1) * 512],
                        start=False, stop=False), deps=[am1, l_calo])
                    am2 = p.add("tensor", lambda e, bnk=bnk, pr=pr: e.matmul(
                        bnk[0:64, 0:512], sa[:],
                        amov[:, 8 * CCH + pr * 512: 8 * CCH + (pr + 1) * 512],
                        start=False, stop=False), deps=[am1b, a2, l_sa])
                    am2b = p.add("tensor", lambda e, bnk=bnk, pr=pr: e.matmul(
                        bnk[0:64, 0:512], salo[:],
                        amov[:, 8 * CCH + pr * 512: 8 * CCH + (pr + 1) * 512],
                        start=False, stop=True), deps=[am2, l_salo])
                    oev = p.add("vector", lambda e, bnk=bnk, jg=jg, pr=pr: e.tensor_copy(
                        ostg[:, (jg * 8 + pr * 2) * CCH:(jg * 8 + pr * 2 + 2) * CCH],
                        bnk[0:64, 0:512]), deps=[am2b, pv.get(("odma", cc - 1))])
                    pv[("apev2", jg, pr)] = oev
                pv[("apev", jg)] = pv[("apev2", jg, 3)]
            o4 = o_hbm[0:NTOK, :].rearrange("(j1 j2) (cb c) -> j1 j2 cb c", j2=N2, c=CCH)
            pv[("odma", cc)] = p.add("sync", lambda e, cc=cc: e.dma_start(
                out=o4[:, :, cc, :],
                in_=ostg[:].rearrange("j (j2 c) -> j j2 c", c=CCH)),
                deps=[pv[("apev", jg2)] for jg2 in range(8)], dma=True)
            pv["eaprev"] = pv.pop("eacc", [])


    if PHASE_LIMIT >= 5 and 5 not in KSKIP:
        p.barrier()
        es_ph.close(); es_ph = _ES()
        gust = sbuf_ph("gust", [128, 4 * INNER], F32)
        xqt = sbuf_ph("xqt", [128, 4 * 1024], BF16)
        wgs = sbuf_ph("wgs", [128, 8 * 512], BF16)
        wus = sbuf_ph("wus", [128, 8 * 512], BF16)
        wds = sbuf_ph("wds", [128, 2 * 4 * 1024], BF16)
        guqT = sbuf_ph("guqT", [128, 32 * 128], BF16)
        gsb = sbuf_ph("gsb", [128, 512], BF16)
        xqT4 = xqT_hbm.rearrange("(kc q) t -> kc q t", q=128)
        wgT4 = wgT_hbm.rearrange("(kc q) t -> kc q t", q=128)
        wuT4 = wuT_hbm.rearrange("(kc q) t -> kc q t", q=128)
        wdT4 = wdT_hbm.rearrange("(ic q) t -> ic q t", q=128)
        ffn3 = ffn_hbm.rearrange("(a q) c -> a q c", q=128)
        fv = {}
        for grp in range(4):
            tiles = [4 * grp + i for i in range(4)]
            lxs = []
            for i in range(4):
                m_ = 4 * grp + i
                lxs.append(p.add("sync", lambda e, i=i, m_=m_: e.dma_start(
                    out=xqt[:, i * 1024:(i + 1) * 1024]
                    .rearrange("q (kc t) -> q kc t", t=128),
                    in_=xqT4[:, :, 128 * m_:128 * m_ + 128]
                    .rearrange("kc q t -> q kc t")),
                    deps=[fv.get(("gmmlast", grp - 1))], dma=True))
            lx = lxs[-1]
            for ss in range(8):
                lwg = p.add("sync", lambda e, ss=ss: e.dma_start(
                    out=wgs[:].rearrange("q (kc c) -> q kc c", c=512),
                    in_=wgT4[:, :, ss * 512:(ss + 1) * 512]
                    .rearrange("kc q c -> q kc c")),
                    deps=[fv.get("wgs_last")], dma=True)
                lwu = p.add("scalar", lambda e, ss=ss: e.dma_start(
                    out=wus[:].rearrange("q (kc c) -> q kc c", c=512),
                    in_=wuT4[:, :, ss * 512:(ss + 1) * 512]
                    .rearrange("kc q c -> q kc c")),
                    deps=[fv.get("wus_last")], dma=True)
                for i, m in enumerate(tiles):
                    if ss == 0:
                        gcol = p.add("vector", lambda e, i=i, m=m: e.scalar_tensor_tensor(
                            colbuf[:, 0 + i:1 + i], dcol_sb[:, m:m + 1], 1.0,
                            wcol[:, 0:1], ALU.mult, ALU.mult),
                            deps=[fv.get(("guused", grp - 1, i))], small=True)
                        ucol = p.add("vector", lambda e, i=i, m=m: e.scalar_tensor_tensor(
                            colbuf[:, 4 + i:5 + i], dcol_sb[:, m:m + 1], 1.0,
                            wcol[:, 1:2], ALU.mult, ALU.mult), deps=[], small=True)
                        fv[("gcol", i)] = gcol
                        fv[("ucol", i)] = ucol
                    gm = None
                    for kc in range(8):
                        gm = p.add("tensor", lambda e, i=i, kc=kc, ss=ss: e.matmul(
                            pb[0][:], xqt[:, i * 1024 + kc * 128:i * 1024 + kc * 128 + 128],
                            wgs[:, kc * 512:(kc + 1) * 512],
                            start=(kc == 0), stop=(kc == 7)),
                            deps=lxs + [lwg, fv.get(("gev", ss, i - 1) if i else ("gev", ss - 1, 3))])
                    um = None
                    for kc in range(8):
                        um = p.add("tensor", lambda e, i=i, kc=kc, ss=ss: e.matmul(
                            pb[1][:], xqt[:, i * 1024 + kc * 128:i * 1024 + kc * 128 + 128],
                            wus[:, kc * 512:(kc + 1) * 512],
                            start=(kc == 0), stop=(kc == 7)),
                            deps=lxs + [lwu, fv.get(("guev", ss, i - 1) if i else ("guev", ss - 1, 3))])
                    gev = p.add("scalar", lambda e, i=i: e.activation(
                        gsb[:], pb[0][:], ACT.Silu, scale=colbuf[:, 0 + i:1 + i]),
                        deps=[gm, fv.get(("gcol", i)),
                              fv.get(("guev", ss, i - 1) if i else ("guev", ss - 1, 3))])
                    fv[("gev", ss, i)] = gev
                    guev = p.add("vector", lambda e, i=i, ss=ss: e.scalar_tensor_tensor(
                        gust[:, i * INNER + ss * 512:i * INNER + (ss + 1) * 512],
                        pb[1][:], colbuf[:, 4 + i:5 + i], gsb[:], ALU.mult, ALU.mult),
                        deps=[um, gev, fv.get(("ucol", i)), fv.get(("guused", grp - 1, i))])
                    fv[("guev", ss, i)] = guev
                    fv[("guev2", ss, i)] = guev
                fv["wgs_last"] = fv[("gev", ss, 3)]
                fv["wus_last"] = fv[("guev", ss, 3)]
            fv[("gmmlast", grp)] = fv[("guev", 7, 3)]
            for i, m in enumerate(tiles):
                guall = [fv[("guev", ss2, i)] for ss2 in range(8)]
                rm = p.add("vector", lambda e, i=i: e.reduce_max(
                    colbuf[:, 8 + i:9 + i], gust[:, i * INNER:(i + 1) * INNER],
                    axis=AX, apply_absolute_value=True), deps=guall, small=True)
                mx = p.add("vector", lambda e, i=i: e.tensor_scalar(
                    colbuf[:, 12 + i:13 + i], colbuf[:, 8 + i:9 + i], 1e-5, None,
                    ALU.max), deps=[rm], small=True)
                rc = p.add("vector", lambda e, i=i: e.reciprocal(
                    colbuf[:, 16 + i:17 + i], colbuf[:, 12 + i:13 + i]),
                    deps=[mx], small=True)
                s2 = p.add("vector", lambda e, i=i: e.tensor_scalar(
                    colbuf[:, 20 + i:21 + i], colbuf[:, 16 + i:17 + i], 127.0, None,
                    ALU.mult), deps=[rc], small=True)
                dn = p.add("vector", lambda e, i=i: e.tensor_scalar(
                    colbuf[:, 24 + i:25 + i], colbuf[:, 12 + i:13 + i], r127c[:, 0:1],
                    wcol[:, 2:3], ALU.mult, ALU.mult), deps=[mx, r1b], small=True)
                q = quant3(stgb[:, 0:INNER], gust[:, i * INNER:(i + 1) * INNER],
                           colbuf[:, 20 + i:21 + i], -128.0, 127.0,
                           [s2, fv.get("qtrlast"), fv.get("dmmlast")] + guall)
                fv[("guused", grp, i)] = q
                trs = {}
                for blk in range(32):
                    bank = pbb[blk % 2]
                    tr = p.add("tensor", lambda e, bk=bank, blk=blk: e.transpose(
                        bk[:, 0:128], stgb[:, blk * 128:blk * 128 + 128], identb[:]),
                        deps=[q, l_idb, trs.get(blk - 2)])
                    ev = p.add("vector", lambda e, bk=bank, blk=blk: e.tensor_copy(
                        guqT[:, blk * 128:blk * 128 + 128], bk[:, 0:128]), deps=[tr])
                    trs[blk] = ev
                fv["qtrlast"] = trs[31]
                dmms = []
                for bat in range(8):
                    slot = wds[:, (bat % 2) * 4096:(bat % 2) * 4096 + 4096]
                    lwd = p.add("sync", lambda e, sl=slot, bat=bat: e.dma_start(
                        out=sl.rearrange("q (ic c) -> q ic c", c=1024),
                        in_=wdT4[4 * bat:4 * bat + 4].rearrange("ic q c -> q ic c")),
                        deps=[fv.get(("dmm", bat - 2))], dma=True)
                    dm = None
                    for ici in range(4):
                        ic = 4 * bat + ici
                        for half in range(2):
                            dm = p.add("tensor", lambda e, ic=ic, ici=ici, half=half,
                                       sl=slot: e.matmul(
                                pb[2 + half][:], guqT[:, ic * 128:(ic + 1) * 128],
                                sl[:, ici * 1024 + half * 512:ici * 1024 + half * 512 + 512],
                                start=(ic == 0), stop=(ic == 31)),
                                deps=[lwd, trs[ic], fv.get(("dev", half))])
                            if ic == 31:
                                dmms.append(dm)
                    fv[("dmm", bat)] = dm
                devs = []
                for half in range(2):
                    ev = p.add("vector", lambda e, half=half, i=i: e.tensor_scalar(
                        stg[:, half * 512:(half + 1) * 512], pb[2 + half][:],
                        colbuf[:, 24 + i:25 + i], None, ALU.mult),
                        deps=[dmms[half], dn, fv.get("ffndma")], small=False)
                    fv[("dev", half)] = ev
                    devs.append(ev)
                fv["dmmlast"] = dmms[-1]
                fv["ffndma"] = p.add("sync", lambda e, m=m: e.dma_start(
                    out=ffn3[m], in_=stg[:, 0:1024]), deps=devs, dma=True)

    if PHASE_LIMIT >= 6 and 6 not in KSKIP:
        p.barrier()
        es_ph.close(); es_ph = _ES()
        uot = sbuf_ph("uot", [128, 1024], BF16)
        uoT = sbuf_ph("uoT", [128, 8 * 128], BF16)
        oown = sbuf_ph("oown", [128, OTILE * 1024], BF16)
        u3r = u_hbm.rearrange("(a q) c -> a q c", q=128)
        o3r = o_hbm.rearrange("(a q) c -> a q c", q=128)
        xo3r = x_own.rearrange("(a q) c -> a q c", q=128)
        os3r = out_sh.rearrange("(a q) c -> a q c", q=128)
        ffn3r = ffn_hbm.rearrange("(a q) c -> a q c", q=128)
        gv = {}
        out_dmas = []
        regcell = {}

        KDYN = os.environ.get("KDYN", "1")

        def _lo_all(e):
            if KDYN == "0":
                src = o_hbm[0:OWN, :].rearrange("(i q) c -> q i c", q=128)
            else:
                ownh = e.snap(e.partition_id() % 2, min_val=0, max_val=1)
                src = o_hbm[bass.ts(ownh, OWN), :].rearrange("(i q) c -> q i c", q=128)
            return e.dma_start(
                out=oown[:].rearrange("q (i c) -> q i c", c=1024), in_=src)
        lo_all = p.add("gpsimd" if os.environ.get("KDYNG", "1") == "1" else "sync",
                       _lo_all, deps=[], dma=True)
        for m in range(OTILE):
            lu = p.add("sync", lambda e, m=m: e.dma_start(
                out=stgb[:, 0:1024], in_=u3r[m]), deps=[gv.get(("uo", m - 1))], dma=True)
            uo = p.add("vector", lambda e, m=m: e.tensor_tensor(
                uot[:], stgb[:, 0:1024], oown[:, m * 1024:(m + 1) * 1024], ALU.mult),
                deps=[lu, lo_all, gv.get(("tr", m - 1, 7))])
            gv[("uo", m)] = uo
            for blk in range(8):
                bank = pbb[blk % 2]
                tr = p.add("tensor", lambda e, bk=bank, blk=blk: e.transpose(
                    bk[:, 0:128], uot[:, blk * 128:blk * 128 + 128], identb[:]),
                    deps=[uo, l_idb, gv.get(("tr", m, blk - 2)),
                          gv.get(("tr", m - 1, blk + 6 - 8) if blk < 2 else None)
                          if False else (gv.get(("tr", m - 1, 6 + blk)) if blk < 2 else None)])
                ev = p.add("vector", lambda e, bk=bank, blk=blk: e.tensor_copy(
                    uoT[:, blk * 128:blk * 128 + 128], bk[:, 0:128]),
                    deps=[tr, gv.get(("wmm", m - 1, blk))])
                gv[("tr", m, blk)] = ev
            lx2 = p.add("gpsimd", lambda e, m=m: e.dma_start(
                out=stg[:, 1024:2048], in_=xo3r[m]), deps=[gv.get(("fadd", m - 1))],
                dma=True)
            lf2 = p.add("gpsimd", lambda e, m=m: e.dma_start(
                out=stg[:, 1024:2048], in_=ffn3r[m], accum_op=ALU.add),
                deps=[lx2], dma=True)
            wmms = []
            for half in range(2):
                wm = None
                for kc in range(8):
                    wm = p.add("tensor", lambda e, kc=kc, half=half: e.matmul(
                        pb[4 + half][:], uoT[:, kc * 128:(kc + 1) * 128],
                        wo_sb[:, kc * 1024 + half * 512:kc * 1024 + half * 512 + 512],
                        start=(kc == 0), stop=(kc == 7)),
                        deps=[gv[("tr", m, kc)]] + l_wo +
                        [gv.get(("fadd2", m - 1, half))])
                    gv[("wmm", m, kc)] = wm
                wmms.append(wm)
            fadds = []
            for half in range(2):
                fa = p.add("vector", lambda e, half=half: e.tensor_tensor(
                    stg[:, half * 512:(half + 1) * 512],
                    stg[:, 1024 + half * 512:1024 + half * 512 + 512],
                    pb[4 + half][:], ALU.add),
                    deps=[wmms[half], lf2, gv.get(("odma2", m - 1))])
                gv[("fadd2", m, half)] = fa
                fadds.append(fa)
            gv[("fadd", m)] = fadds[-1]
            od = p.add("sync", lambda e, m=m: e.dma_start(
                out=os3r[m], in_=stg[:, 0:1024]), deps=fadds, dma=True)
            gv[("odma2", m)] = od
            out_dmas.append(od)
        p.add("sync", lambda e: e.nop(), deps=out_dmas)

    p.barrier()
    if PHASE_LIMIT < 6 or 6 in KSKIP:
        zo = p.add("vector", lambda e: e.memset(stg[:, 0:1024], 0.0),
                   deps=[len(p.ops) - 1])
        os3 = out_sh.rearrange("(a q) c -> a q c", q=128)
        fin_dmas = [p.add("sync", lambda e, m=m: e.dma_start(out=os3[m], in_=stg[:, 0:1024]),
                          deps=[zo], dma=True) for m in range(OTILE)]
        p.add("sync", lambda e: e.nop(), deps=fin_dmas)

    es_ph.close()
    emit_engine, dcount, ccount = p.lower(sems_c, sems_d)
    with es:
        with nc.Block() as block:
            @block.sync
            def _(eng):
                emit_engine("sync", eng)

            @block.scalar
            def _(eng):
                emit_engine("scalar", eng)

            @block.tensor
            def _(eng):
                emit_engine("tensor", eng)

            @block.vector
            def _(eng):
                emit_engine("vector", eng)

            @block.gpsimd
            def _(eng):
                emit_engine("gpsimd", eng)
    print(f"[kernel] ops={len(p.ops)} dma={sum(dcount.values())//16} "
          f"compute={sum(ccount.values())}")
    return nc


def make_in_maps(inputs):
    x = np.ascontiguousarray(inputs["L0_attn_x"], np.float32)
    base = {
        "wuv": np.ascontiguousarray(inputs["L0_attn_W_uv"], np.float32),
        "wo": np.ascontiguousarray(inputs["L0_attn_W_o"], np.float32),
        "rpe_in": np.ascontiguousarray(inputs["L0_attn_rpe_in_w"], np.float32),
        "rpe_hid": np.ascontiguousarray(inputs["L0_attn_rpe_hid_w"], np.float32
                                        ).reshape(RPE_L * FEAT, FEAT),
        "rpe_g": np.ascontiguousarray(inputs["L0_attn_rpe_ln_g"], np.float32),
        "rpe_b": np.ascontiguousarray(inputs["L0_attn_rpe_ln_b"], np.float32),
        "rpe_out": np.ascontiguousarray(inputs["L0_attn_rpe_out_w"], np.float32),
        "dgamma": np.ascontiguousarray(inputs["L0_attn_decay_gamma"], np.float32
                                       ).reshape(1, D),
        "wg_in": np.ascontiguousarray(inputs["L0_ffn_w_gate"], np.float32),
        "wu_in": np.ascontiguousarray(inputs["L0_ffn_w_up"], np.float32),
        "wd_in": np.ascontiguousarray(inputs["L0_ffn_w_down"], np.float32),
    }
    in_maps = []
    for core in range(8):
        b, h = core // 2, core % 2
        m = dict(base)
        m["x_full"] = np.ascontiguousarray(x[b])
        m["x_own"] = np.ascontiguousarray(x[b, h * OWN:(h + 1) * OWN])
        in_maps.append(m)
    return in_maps


def kernel(**inputs):
    nc = build_program()
    in_maps = make_in_maps(inputs)
    res = run_bass_kernel_spmd(nc, in_maps, core_ids=list(range(8)))
    out = np.empty((4, NTOK, D), np.float32)
    for core in range(8):
        b, h = core // 2, core % 2
        out[b, h * OWN:(h + 1) * OWN] = res.results[core]["out_sh"]
    return out


if __name__ == "__main__":
    build_program()
